# version 04
# baseline (speedup 1.0000x reference)
import contextlib
import numpy as np

import concourse.bass as bass
import concourse.mybir as mybir
from concourse.bass_utils import run_bass_kernel_spmd

F32 = mybir.dt.float32
BF16 = mybir.dt.bfloat16
FP8 = mybir.dt.float8e4
U8 = mybir.dt.uint8
AF = mybir.ActivationFunctionType
ALU = mybir.AluOpType
AX = mybir.AxisListType

D = 2048
ND = 16
CH = 256
TT = 512
LN_EPS = 1e-5
RMS_EPS = float(np.finfo(np.float32).eps)
SLOT = 4096
SKIP = set()
PHASES = None

ENGS = ("PE", "ACT", "DVE", "SP", "POOL")
SEMS = {"sPE": 1, "sACT": 1, "sDVE": 1, "sPOOL": 1,
        "dSP": 16, "dACT": 16, "dPOOL": 16, "sCC": 1}
ENG_COMPUTE_SEM = {"PE": "sPE", "ACT": "sACT", "DVE": "sDVE", "POOL": "sPOOL"}
ENG_DMA_SEM = {"SP": "dSP", "ACT": "dACT", "POOL": "dPOOL"}


class KB:

    enabled = True

    def __init__(self, nc):
        self.nc = nc
        self.ops = {e: [] for e in ENGS}
        self.cnt = {s: 0 for s in SEMS}
        self.waited = {}
        self.buf = {}
        self.semh = {}
        self.n_ops = 0

    def init_sems(self, stack):
        for s in SEMS:
            self.semh[s] = stack.enter_context(self.nc.semaphore(s))

    def _wait(self, eng, sem, val):
        if val <= 0:
            return
        key = (eng, sem)
        if self.waited.get(key, 0) >= val:
            return
        self.waited[key] = val
        h = self.semh[sem]
        self.ops[eng].append(lambda E, h=h, v=val: E.wait_ge(h, v))

    last_tr = None

    def emit(self, eng, fn, reads=(), writes=(), dma=False, cc=False,
             transpose=False):
        if not self.enabled:
            return
        if cc:
            sem = "sCC"
        elif dma:
            sem = ENG_DMA_SEM[eng]
        else:
            sem = ENG_COMPUTE_SEM[eng]
        if transpose:
            for ds in ("dSP", "dACT", "dPOOL"):
                if ds != sem:
                    self._wait(eng, ds, self.cnt[ds])
        elif dma and self.last_tr is not None:
            self._wait(eng, self.last_tr[0], self.last_tr[1])
        my_compute_sem = "sPE" if (eng == "PE" and not dma and not cc) else None
        need = {}

        def add_need(s, v):
            if s == my_compute_sem:
                return
            if need.get(s, 0) < v:
                need[s] = v

        for k in reads:
            st = self.buf.get(k)
            if st and st["w"]:
                add_need(*st["w"])
        for k in writes:
            st = self.buf.get(k)
            if st:
                if st["w"]:
                    add_need(*st["w"])
                for s, v in st["r"].items():
                    add_need(s, v)
        for s in sorted(need):
            self._wait(eng, s, need[s])
        inc = SEMS[sem]
        self.cnt[sem] += inc
        val = self.cnt[sem]
        h = self.semh[sem]
        self.ops[eng].append(lambda E, fn=fn, h=h, i=inc: fn(E).then_inc(h, i))
        self.n_ops += 1
        if transpose:
            self.last_tr = (sem, val)
        for k in writes:
            self.buf[k] = {"w": (sem, val), "r": {}}
        for k in reads:
            st = self.buf.setdefault(k, {"w": None, "r": {}})
            if st["r"].get(sem, 0) < val:
                st["r"][sem] = val

    def barrier_wait(self, eng):
        for s in SEMS:
            self._wait(eng, s, self.cnt[s])

    def finalize(self, block):
        for eng, attr in (("PE", "tensor"), ("ACT", "scalar"), ("DVE", "vector"),
                          ("SP", "sync"), ("POOL", "gpsimd")):
            lst = self.ops[eng]

            def body(E, lst=lst):
                for f in lst:
                    f(E)
            getattr(block, attr)(body)


def build_program(n_tiles=8, debug_scratch=False, max_phase=9):
    n_tok = n_tiles * TT
    n_ch = n_tok // CH
    nc = bass.Bass("TRN2", target_bir_lowering=False,
                   detect_race_conditions=False)
    SCR = "ExternalOutput" if debug_scratch else "Internal"

    x_d = nc.dram_tensor("x", [n_tok, D], F32, kind="ExternalInput").ap()
    wq_d = nc.dram_tensor("wq_t", [D, D], F32, kind="ExternalInput").ap()
    wk_d = nc.dram_tensor("wk_t", [D, D], F32, kind="ExternalInput").ap()
    wv_d = nc.dram_tensor("wv_t", [D, D], F32, kind="ExternalInput").ap()
    wg_d = nc.dram_tensor("wg_t", [D, D], F32, kind="ExternalInput").ap()
    wp_d = nc.dram_tensor("wp_t", [D, D], F32, kind="ExternalInput").ap()
    w2_d = nc.dram_tensor("w2", [D, D], F32, kind="ExternalInput").ap()
    scv_d = nc.dram_tensor("scv", [128, 1], F32, kind="ExternalInput").ap()
    sc1v_d = nc.dram_tensor("sc1v", [128, 1], F32, kind="ExternalInput").ap()
    cg_d = nc.dram_tensor("cgate", [128, 1], F32, kind="ExternalInput").ap()
    mask_d = nc.dram_tensor("masks", [128, 512], F32, kind="ExternalInput").ap()
    eye_d = nc.dram_tensor("eye", [128, 128], F32, kind="ExternalInput").ap()
    out_d = nc.dram_tensor("out", [n_tok, D], F32, kind="ExternalOutput").ap()

    xr_s = nc.dram_tensor("xr_s", [n_tok, D], BF16, kind=SCR).ap()
    x8w_s = nc.dram_tensor("x8w_s", [n_tok, D], BF16, kind=SCR).ap()
    kf_s = nc.dram_tensor("kf_s", [D, n_tok], BF16, kind=SCR).ap()
    vf_s = nc.dram_tensor("vf_s", [D, n_tok], BF16, kind=SCR).ap()
    q_s = nc.dram_tensor("q_s", [n_ch, 128, ND * CH], BF16, kind=SCR).ap()
    k_s = nc.dram_tensor("k_s", [n_ch, 128, ND * CH], BF16, kind=SCR).ap()
    g_s = nc.dram_tensor("g_s", [n_ch, 128, ND * CH], BF16, kind=SCR).ap()
    o_s = nc.dram_tensor("o_s", [n_ch, 128, ND * CH], BF16, kind=SCR).ap()
    og_s = nc.dram_tensor("og_s", [n_ch, 128, ND * CH], BF16, kind=SCR).ap()
    agi_s = nc.dram_tensor("agi_s", [D, D], BF16, kind="Internal").ap()
    ago_s = nc.dram_tensor("ago_s", [2 * D, D], BF16, kind="Internal").ap()
    ago_dump = (nc.dram_tensor("ago_dump", [2 * D, D], BF16,
                               kind="ExternalOutput").ap()
                if debug_scratch else None)

    with contextlib.ExitStack() as st:
        e = st.enter_context
        kb = KB(nc)
        kb.init_sems(st)

        W1 = e(nc.sbuf_tensor("W1", [128, 64 * 1024], U8)).ap()
        W2 = e(nc.sbuf_tensor("W2", [128, 64 * 1024], U8)).ap()
        PACT = e(nc.sbuf_tensor("PACT", [128, 36 * 1024], U8)).ap()

        BPE = {F32: 4, BF16: 2, FP8: 1, U8: 1}

        def view(pool, pool_name, off_bytes, dt_, cols):
            nb = cols * BPE[dt_]
            ap = pool[:, off_bytes:off_bytes + nb].bitcast(dt_)
            s0 = off_bytes // SLOT
            s1 = (off_bytes + nb - 1) // SLOT
            keys = [(pool_name, s) for s in range(s0, s1 + 1)]
            return ap, keys

        def pact_keys(off_bytes, nb):
            s0 = off_bytes // SLOT
            s1 = (off_bytes + nb - 1) // SLOT
            return [("PACT", s) for s in range(s0, s1 + 1)]

        def w2_keys(off_bytes, nb):
            s0 = off_bytes // SLOT
            s1 = (off_bytes + nb - 1) // SLOT
            return [("W2", s) for s in range(s0, s1 + 1)]

        mask_sb = e(nc.sbuf_tensor("mask_sb", [128, 512], BF16))
        scv = e(nc.sbuf_tensor("scv_sb", [128, 1], F32))
        sc1v = e(nc.sbuf_tensor("sc1v_sb", [128, 1], F32))
        cgv = e(nc.sbuf_tensor("cg_sb", [128, 1], F32))
        inv = e(nc.sbuf_tensor("inv_sb", [128, 1], F32))
        invt = e(nc.sbuf_tensor("invt_sb", [128, 1], F32))
        stt = e(nc.sbuf_tensor("stats", [128, 64], F32))
        st8 = e(nc.sbuf_tensor("st8", [128, 512], BF16))
        osb = e(nc.sbuf_tensor("osb", [128, 2 * CH], BF16))
        qkb = e(nc.sbuf_tensor("qkb", [128, 4 * TT], BF16))
        ogb = e(nc.sbuf_tensor("ogb", [128, 4 * CH], BF16))
        t1b = e(nc.sbuf_tensor("t1b", [128, 2 * 512], F32))
        x8t_p = [e(nc.sbuf_tensor("x8t_p0", [128, 2 * D], BF16)),
                 e(nc.sbuf_tensor("x8t_p1", [128, 2 * D], BF16))]
        x8s_t = e(nc.sbuf_tensor("x8s_t", [128, D], FP8))

        PA = e(nc.psum_tensor("PA", [128, 1024], F32)).ap()
        PB = e(nc.psum_tensor("PB", [128, 1024], F32)).ap()
        PC = e(nc.psum_tensor("PC", [128, 512], F32)).ap()
        PD = e(nc.psum_tensor("PD", [128, 512], F32)).ap()
        PE2 = e(nc.psum_tensor("PE2", [128, 512], F32)).ap()

        blk = e(nc.Block())
        EM = kb.emit

        EM("POOL", lambda E: E.dma_start(out=mask_sb[:], in_=mask_d[:, :]),
           writes=["mask"], dma=True)
        EM("SP", lambda E: E.dma_start(out=scv[:], in_=scv_d[:, :]),
           writes=["scv"], dma=True)
        EM("SP", lambda E: E.dma_start(out=sc1v[:], in_=sc1v_d[:, :]),
           writes=["sc1v"], dma=True)
        EM("SP", lambda E: E.dma_start(out=cgv[:], in_=cg_d[:, :]),
           writes=["cgv"], dma=True)
        EM("DVE", lambda E: E.tensor_mul(invt[:], scv[:], sc1v[:]),
           reads=["scv", "sc1v"], writes=["invt"])
        EM("DVE", lambda E: E.reciprocal(inv[:], invt[:]),
           reads=["invt"], writes=["inv"])

        n_sub = n_tok // 128
        xb, _ = view(PACT, "PACT", 0, F32, 2 * D)
        xsq, _ = view(PACT, "PACT", 16 * 1024, F32, D)
        xrb, _ = view(PACT, "PACT", 24 * 1024, BF16, 2 * D)
        x8b, _ = view(PACT, "PACT", 32 * 1024, FP8, 2 * D)
        x8w, _ = view(PACT, "PACT", 16 * 1024, BF16, 2 * D)

        EM("ACT", lambda E: E.preload_act_table(AF.Sqrt))
        for s in range(n_sub if "p0" not in SKIP else 0):
            par = s % 2
            xv = xb[:, par * D:(par + 1) * D]
            xk = pact_keys(par * 8 * 1024, 8 * 1024)
            sqv = xsq[:, 0:D]
            sqk = pact_keys(16 * 1024, 8 * 1024)
            xrv = xrb[:, par * D:(par + 1) * D]
            xrk = pact_keys(24 * 1024 + par * 4 * 1024, 4 * 1024)
            x8v = x8b[:, par * D:(par + 1) * D]
            x8k = pact_keys(32 * 1024 + par * 2 * 1024, 2 * 1024)
            EM("SP", lambda E, s=s, xv=xv: E.dma_start(
                out=xv, in_=x_d[s * 128:(s + 1) * 128, :]),
               writes=xk, dma=True)
            c0 = par * 8
            s1v = stt[:, c0:c0 + 1]
            m = stt[:, c0 + 1:c0 + 2]
            ex2 = stt[:, c0 + 2:c0 + 3]
            var = stt[:, c0 + 3:c0 + 4]
            ta = stt[:, c0 + 4:c0 + 5]
            tb = stt[:, c0 + 5:c0 + 6]
            aa = stt[:, c0 + 6:c0 + 7]
            bb = stt[:, c0 + 7:c0 + 8]
            sk = [("stt", par)]
            EM("DVE", lambda E, xv=xv, s1v=s1v: E.reduce_sum(s1v, xv, AX.X),
               reads=xk, writes=sk)
            EM("ACT", lambda E, xv=xv, sqv=sqv: E.activation(sqv, xv, AF.Square),
               reads=xk, writes=sqk)
            EM("DVE", lambda E, sqv=sqv, ex2=ex2: E.reduce_sum(ex2, sqv, AX.X),
               reads=sqk, writes=sk)

            def dve(fn):
                EM("DVE", fn, reads=sk, writes=sk)
            dve(lambda E, m=m, s1v=s1v: E.tensor_scalar_mul(m, s1v, 1.0 / D))
            dve(lambda E, ex2=ex2: E.tensor_scalar_mul(ex2, ex2, 1.0 / D))
            dve(lambda E, var=var, m=m: E.tensor_mul(var, m, m))
            dve(lambda E, var=var, ex2=ex2: E.tensor_sub(var, ex2, var))
            dve(lambda E, ta=ta, var=var: E.tensor_scalar_add(ta, var, LN_EPS))
            dve(lambda E, ta=ta: E.reciprocal(ta, ta))
            dve(lambda E, tb=tb, m=m: E.tensor_scalar_mul(tb, m, -1.0))
            EM("ACT", lambda E, ta=ta, aa=aa: E.activation(aa, ta, AF.Sqrt),
               reads=sk, writes=sk)
            dve(lambda E, ta=ta, var=var: E.tensor_mul(ta, var, ta))
            dve(lambda E, ta=ta: E.tensor_scalar_add(ta, ta, RMS_EPS))
            dve(lambda E, ta=ta: E.reciprocal(ta, ta))
            EM("ACT", lambda E, ta=ta, bb=bb: E.activation(bb, ta, AF.Sqrt),
               reads=sk, writes=sk)
            dve(lambda E, bb=bb, aa=aa: E.tensor_mul(bb, aa, bb))
            dve(lambda E, ta=ta, tb=tb, bb=bb: E.tensor_mul(ta, tb, bb))
            EM("DVE", lambda E, aa=aa: E.tensor_mul(aa, aa, scv[:]),
               reads=sk + ["scv"], writes=sk)
            dve(lambda E, tb=tb, aa=aa: E.tensor_mul(tb, tb, aa))
            EM("ACT", lambda E, xv=xv, xrv=xrv, bb=bb, ta=ta: E.activation(
                xrv, xv, AF.Identity, bias=ta, scale=bb),
               reads=xk + sk, writes=xrk)
            EM("DVE", lambda E, xv=xv, x8v=x8v, aa=aa, tb=tb: E.tensor_scalar(
                x8v, xv, aa, tb, ALU.mult, ALU.add),
               reads=xk + sk, writes=x8k)
            EM("SP", lambda E, s=s, xrv=xrv: E.dma_start(
                out=xr_s[s * 128:(s + 1) * 128, :], in_=xrv),
               reads=xrk, writes=[("xr_s", s)], dma=True)
            x8wv = x8w[:, par * D:(par + 1) * D]
            x8wk = pact_keys(16 * 1024 + par * 4 * 1024, 4 * 1024)
            EM("DVE", lambda E, x8v=x8v, x8wv=x8wv: E.tensor_copy(x8wv, x8v),
               reads=x8k, writes=x8wk)
            EM("POOL", lambda E, s=s, x8wv=x8wv: E.dma_start(
                out=x8w_s[s * 128:(s + 1) * 128, :], in_=x8wv),
               reads=x8wk, writes=[("x8w_s", s)], dma=True)

        def load_weight_bf16(wd, dst, dst_keys):
            for dc in range(ND):
                EM("POOL", lambda E, dc=dc, dst=dst, wd=wd: E.dma_start(
                    out=dst[:, dc * D:(dc + 1) * D],
                    in_=wd[dc * 128:(dc + 1) * 128, :]),
                   writes=dst_keys, dma=True)

        w1v, w1k = view(W1, "W1", 0, BF16, ND * D)
        w2v, w2k = view(W2, "W2", 0, BF16, ND * D)
        xrf, _ = view(PACT, "PACT", 0, BF16, 2 * ND * TT)

        def load_xrf(t, par):
            v = xrf[:, par * ND * TT:(par + 1) * ND * TT]
            keys = pact_keys(par * ND * TT * 2, ND * TT * 2)
            for dc in range(ND):
                EM("SP", lambda E, t=t, dc=dc, v=v: E.dma_start_transpose(
                    v[:, dc * TT:(dc + 1) * TT],
                    xr_s[t * TT:(t + 1) * TT, dc * 128:(dc + 1) * 128]),
                   reads=[("xr_s", 4 * t + i) for i in range(4)],
                   writes=keys, dma=True, transpose=True)
            return v, keys

        psA = [(PA[:, 0:512], [("PA", 0)]), (PA[:, 512:1024], [("PA", 1)]),
               (PB[:, 0:512], [("PB", 0)]), (PB[:, 512:1024], [("PB", 1)])]

        def proj_phase(specs):
            for t in range(n_tiles):
                par = t % 2
                xv, xkeys = load_xrf(t, par)
                for wi, (wsb, wkeys, act_fn, pm, pm_name, fm, fm_name) \
                        in enumerate(specs):
                    for oc in range(ND):
                        r = (wi * ND + oc) % 4
                        ps, pskeys = psA[r]

                        def mms(E, oc=oc, ps=ps, xv=xv, wsb=wsb):
                            for dc in range(ND):
                                inst = E.matmul(
                                    ps,
                                    wsb[:, dc * D + oc * 128:dc * D + (oc + 1) * 128],
                                    xv[:, dc * TT:(dc + 1) * TT],
                                    start=(dc == 0), stop=(dc == ND - 1))
                            return inst
                        EM("PE", mms, reads=wkeys + xkeys, writes=pskeys)
                        qv = qkb[:, r * TT:(r + 1) * TT]
                        EM("ACT", lambda E, qv=qv, ps=ps, act_fn=act_fn:
                           E.activation(qv, ps, act_fn),
                           reads=pskeys, writes=[("qkb", r)])
                        if pm is not None:
                            for j in range(2):
                                EM("SP", lambda E, t=t, j=j, oc=oc, qv=qv, pm=pm:
                                   E.dma_start(
                                       out=pm[2 * t + j, :,
                                              oc * CH:(oc + 1) * CH],
                                       in_=qv[:, j * CH:(j + 1) * CH]),
                                   reads=[("qkb", r)],
                                   writes=[(pm_name, 2 * t + j)], dma=True)
                        if fm is not None:
                            EM("SP", lambda E, t=t, oc=oc, qv=qv, fm=fm:
                               E.dma_start(
                                   out=fm[oc * 128:(oc + 1) * 128,
                                          t * TT:(t + 1) * TT],
                                   in_=qv),
                               reads=[("qkb", r)],
                               writes=[(fm_name, t)], dma=True)

        kb.enabled = max_phase >= 1 and (PHASES is None or 1 in PHASES)
        EM("ACT", lambda E: E.preload_act_table(AF.Silu))
        load_weight_bf16(wq_d, w1v, w1k)
        load_weight_bf16(wk_d, w2v, w2k)
        proj_phase([(w1v, w1k, AF.Silu, q_s, "q_s", None, None),
                    (w2v, w2k, AF.Silu, k_s, "k_s", kf_s, "kf_s")])
        load_weight_bf16(wv_d, w1v, w1k)
        proj_phase([(w1v, w1k, AF.Silu, None, None, vf_s, "vf_s")])

        kb.enabled = max_phase >= 2 and (PHASES is None or 2 in PHASES)
        kvv, kv_keys = view(W1, "W1", 0, BF16, ND * D)
        qf, _ = view(W2, "W2", 0, BF16, 2 * ND * CH)
        kf, _ = view(W2, "W2", 16 * 1024, BF16, 2 * ND * CH)
        vt, _ = view(W2, "W2", 32 * 1024, BF16, 2 * 2 * D)
        kt, _ = view(W2, "W2", 48 * 1024, BF16, 2 * 2 * D)

        EM("DVE", lambda E: E.memset(kvv, 0.0), writes=kv_keys)

        for c in range(n_ch):
            par = c % 2
            t = c // 2
            qv = qf[:, par * ND * CH:(par + 1) * ND * CH]
            qk = w2_keys(par * ND * CH * 2, ND * CH * 2)
            kv_ = kf[:, par * ND * CH:(par + 1) * ND * CH]
            kk = w2_keys(16 * 1024 + par * ND * CH * 2, ND * CH * 2)
            vtv = vt[:, par * 2 * D:(par + 1) * 2 * D]
            vtk = w2_keys(32 * 1024 + par * 8 * 1024, 8 * 1024)
            ktv = kt[:, par * 2 * D:(par + 1) * 2 * D]
            ktk = w2_keys(48 * 1024 + par * 8 * 1024, 8 * 1024)
            EM("SP", lambda E, c=c, qv=qv: E.dma_start(
                out=qv, in_=q_s[c, :, :]),
               reads=[("q_s", c)], writes=qk, dma=True)
            EM("ACT", lambda E, c=c, kv_=kv_: E.dma_start(
                out=kv_, in_=k_s[c, :, :]),
               reads=[("k_s", c)], writes=kk, dma=True)
            for jc in range(2):
                EM("SP", lambda E, c=c, jc=jc, vtv=vtv: E.dma_start_transpose(
                    vtv[:, jc * D:(jc + 1) * D],
                    vf_s[:, c * CH + jc * 128:c * CH + (jc + 1) * 128]),
                   reads=[("vf_s", t)], writes=vtk, dma=True, transpose=True)
                EM("SP", lambda E, c=c, jc=jc, ktv=ktv: E.dma_start_transpose(
                    ktv[:, jc * D:(jc + 1) * D],
                    kf_s[:, c * CH + jc * 128:c * CH + (jc + 1) * 128]),
                   reads=[("kf_s", t)], writes=ktk, dma=True, transpose=True)
            for jc in range(2):
                ps = PC[:, 0:CH] if jc == 0 else PD[:, 0:CH]
                pk = ["PC"] if jc == 0 else ["PD"]

                def smm(E, jc=jc, ps=ps, kv_=kv_, qv=qv):
                    for dc in range(ND):
                        inst = E.matmul(
                            ps,
                            kv_[:, dc * CH + jc * 128:dc * CH + (jc + 1) * 128],
                            qv[:, dc * CH:(dc + 1) * CH],
                            start=(dc == 0), stop=(dc == ND - 1))
                    return inst
                EM("PE", smm, reads=kk + qk, writes=pk)
                EM("DVE", lambda E, jc=jc, ps=ps: E.tensor_mul(
                    st8[:, jc * CH:(jc + 1) * CH], ps,
                    mask_sb[:, jc * CH:(jc + 1) * CH]),
                   reads=pk + ["mask"], writes=[("st8", jc)])
            for ec in range(ND):
                ps = PE2[:, 0:CH] if ec % 2 == 0 else PC[:, 0:CH]
                pk = [("PE2",)] if ec % 2 == 0 else ["PC"]

                def omm(E, ps=ps, ec=ec, qv=qv, vtv=vtv):
                    for dc in range(ND):
                        E.matmul(ps,
                                 kvv[:, dc * D + ec * 128:dc * D + (ec + 1) * 128],
                                 qv[:, dc * CH:(dc + 1) * CH],
                                 start=(dc == 0), stop=False)
                    for jc in range(2):
                        inst = E.matmul(
                            ps,
                            vtv[:, jc * D + ec * 128:jc * D + (ec + 1) * 128],
                            st8[:, jc * CH:(jc + 1) * CH],
                            start=False, stop=(jc == 1))
                    return inst
                EM("PE", omm,
                   reads=kv_keys + qk + vtk + [("st8", 0), ("st8", 1)],
                   writes=pk)
                ov = osb[:, (ec % 2) * CH:(ec % 2 + 1) * CH]
                EM("DVE", lambda E, ov=ov, ps=ps: E.tensor_copy(ov, ps),
                   reads=pk, writes=[("osb", ec % 2)])
                EM("ACT", lambda E, c=c, ec=ec, ov=ov: E.dma_start(
                    out=o_s[c, :, ec * CH:(ec + 1) * CH], in_=ov),
                   reads=[("osb", ec % 2)], writes=[("o_s", c)], dma=True)
            for dc in range(ND):
                for es in range(4):
                    ps = PC if (dc * 4 + es) % 2 == 0 else PD
                    pk = ["PC"] if (dc * 4 + es) % 2 == 0 else ["PD"]

                    def kvmm(E, dc=dc, es=es, ps=ps, ktv=ktv, vtv=vtv):
                        for jc in range(2):
                            inst = E.matmul(
                                ps,
                                ktv[:, jc * D + dc * 128:jc * D + (dc + 1) * 128],
                                vtv[:, jc * D + es * 512:jc * D + (es + 1) * 512],
                                start=(jc == 0), stop=(jc == 1))
                        return inst
                    EM("PE", kvmm, reads=ktk + vtk, writes=pk)
                    ksl = kvv[:, dc * D + es * 512:dc * D + (es + 1) * 512]
                    kkey = [("W1", (dc * D * 2 + es * 1024) // SLOT)]
                    EM("DVE", lambda E, ksl=ksl, ps=ps: E.tensor_add(ksl, ksl, ps),
                       reads=pk + kkey, writes=kkey)

        kb.enabled = max_phase >= 3 and (PHASES is None or 3 in PHASES)
        for dc in range(ND):
            EM("POOL", lambda E, dc=dc: E.dma_start(
                out=agi_s[dc * 128:(dc + 1) * 128, :],
                in_=kvv[:, dc * D:(dc + 1) * D]),
               reads=kv_keys, writes=["agi"], dma=True)
        EM("POOL", lambda E: E.collective_compute(
            "AllGather", ALU.bypass,
            replica_groups=[[0, 1], [2, 3], [4, 5], [6, 7]],
            ins=[agi_s.opt()], outs=[ago_s.opt()]),
           reads=["agi"], writes=["ago"], cc=True)
        if ago_dump is not None:
            EM("POOL", lambda E: E.dma_start(out=ago_dump[:, :], in_=ago_s[:, :]),
               reads=["ago"], writes=["ago_dump"], dma=True)

        kb.enabled = max_phase >= 4 and (PHASES is None or 4 in PHASES)
        EM("ACT", lambda E: E.preload_act_table(AF.Sigmoid))
        load_weight_bf16(wg_d, w2v, w2k)
        proj_phase([(w2v, w2k, AF.Sigmoid, g_s, "g_s", None, None)])

        kb.enabled = max_phase >= 5 and (PHASES is None or 5 in PHASES)
        kvp, kvp_keys = view(W1, "W1", 0, BF16, ND * D)
        for dc in range(ND):
            EM("SP", lambda E, dc=dc: E.dma_start(
                out=kvp[:, dc * D:(dc + 1) * D],
                in_=ago_s[dc * 128:(dc + 1) * 128, :]),
               reads=["ago"], writes=kvp_keys, dma=True)
        qc3, _ = view(W2, "W2", 0, BF16, 2 * ND * CH)
        olc, _ = view(W2, "W2", 16 * 1024, BF16, 2 * ND * CH)
        gls, _ = view(W2, "W2", 32 * 1024, BF16, 2 * ND * CH)
        for c in range(n_ch):
            par = c % 2
            qv = qc3[:, par * ND * CH:(par + 1) * ND * CH]
            qk = w2_keys(par * ND * CH * 2, ND * CH * 2)
            olv = olc[:, par * ND * CH:(par + 1) * ND * CH]
            olk = w2_keys(16 * 1024 + par * ND * CH * 2, ND * CH * 2)
            glv = gls[:, par * ND * CH:(par + 1) * ND * CH]
            glk = w2_keys(32 * 1024 + par * ND * CH * 2, ND * CH * 2)
            EM("SP", lambda E, c=c, qv=qv: E.dma_start(
                out=qv, in_=q_s[c, :, :]),
               reads=[("q_s", c)], writes=qk, dma=True)
            EM("ACT", lambda E, c=c, olv=olv: E.dma_start(
                out=olv, in_=o_s[c, :, :]),
               reads=[("o_s", c)], writes=olk, dma=True)
            EM("ACT", lambda E, c=c, glv=glv: E.dma_start(
                out=glv, in_=g_s[c, :, :]),
               reads=[("g_s", c)], writes=glk, dma=True)
            for ec in range(ND):
                ps0, pskeys = psA[ec % 4]
                ps = ps0[:, 0:CH]

                def cmm(E, ec=ec, ps=ps, qv=qv):
                    for dc in range(ND):
                        inst = E.matmul(
                            ps,
                            kvp[:, dc * D + ec * 128:dc * D + (ec + 1) * 128],
                            qv[:, dc * CH:(dc + 1) * CH],
                            start=(dc == 0), stop=(dc == ND - 1))
                    return inst
                EM("PE", cmm, reads=kvp_keys + qk, writes=pskeys)
                ogv = ogb[:, (ec % 4) * CH:(ec % 4 + 1) * CH]
                EM("DVE", lambda E, ec=ec, ps=ps, ogv=ogv, olv=olv:
                   E.scalar_tensor_tensor(
                       ogv, ps, cgv[:], olv[:, ec * CH:(ec + 1) * CH],
                       ALU.mult, ALU.add),
                   reads=pskeys + olk + ["cgv"], writes=[("ogb", ec % 4)])
                EM("DVE", lambda E, ec=ec, ogv=ogv, glv=glv: E.tensor_mul(
                    ogv, ogv, glv[:, ec * CH:(ec + 1) * CH]),
                   reads=glk + [("ogb", ec % 4)], writes=[("ogb", ec % 4)])
                EM("SP", lambda E, c=c, ec=ec, ogv=ogv: E.dma_start(
                    out=og_s[c, :, ec * CH:(ec + 1) * CH], in_=ogv),
                   reads=[("ogb", ec % 4)], writes=[("og_s", c)], dma=True)

        kb.enabled = max_phase >= 6 and (PHASES is None or 6 in PHASES)
        wpv, wp_keys = view(W1, "W1", 0, BF16, ND * D)
        if "wp" not in SKIP:
            load_weight_bf16(wp_d, wpv, wp_keys)
        w28, w28_k = view(W2, "W2", 0, BF16, ND * D)
        w2st, _ = view(PACT, "PACT", 0, F32, D)
        q8buf = x8s_t.ap()[:, 0:D]
        for dc in range(ND if "w28" not in SKIP else 0):
            wst = w2st[:, 0:D]
            wstk = pact_keys(0, 8 * 1024)
            EM("SP", lambda E, dc=dc, wst=wst: E.dma_start(
                out=wst, in_=w2_d[dc * 128:(dc + 1) * 128, :]),
               writes=wstk, dma=True)
            EM("DVE", lambda E, dc=dc, wst=wst: E.tensor_single_scalar(
                q8buf, wst, sc1v[:], ALU.mult),
               reads=wstk + ["sc1v"], writes=["q8buf"])
            EM("DVE", lambda E, dc=dc: E.tensor_copy(
                w28[:, dc * D:(dc + 1) * D], q8buf),
               reads=["q8buf"], writes=w28_k)
        ogi0, ogi0_k = view(PACT, "PACT", 24 * 1024, BF16, ND * CH)
        xres, _ = view(PACT, "PACT", 8 * 1024, F32, D)
        outb, _ = view(PACT, "PACT", 16 * 1024, F32, D)
        xres_k = pact_keys(8 * 1024, 8 * 1024)
        outb_k = pact_keys(16 * 1024, 8 * 1024)

        for c in range(n_ch):
            par = c % 2
            ogi = ogi0
            ogi_k = ogi0_k
            if "ogi" not in SKIP:
                EM("ACT", lambda E, c=c, ogi=ogi: E.dma_start(
                    out=ogi, in_=og_s[c, :, :]),
                   reads=[("og_s", c)], writes=ogi_k, dma=True)
            x8tv = x8t_p[par].ap()
            x8tk = [("x8t_t", par)]
            for dc in range(ND):
                EM("SP", lambda E, c=c, dc=dc, x8tv=x8tv: E.dma_start_transpose(
                    x8tv[:, dc * CH:(dc + 1) * CH],
                    x8w_s[c * CH:(c + 1) * CH, dc * 128:(dc + 1) * 128]),
                   reads=[("x8w_s", 2 * c), ("x8w_s", 2 * c + 1)],
                   writes=x8tk, dma=True, transpose=True)
            for ic in range(2):
                xrv = xres[:, 0:D]
                EM("ACT", lambda E, c=c, ic=ic, xrv=xrv: E.dma_start(
                    out=xrv,
                    in_=x_d[c * CH + ic * 128:c * CH + (ic + 1) * 128, :]),
                   writes=xres_k, dma=True)
                ov = outb[:, 0:D]
                for oh in range(4):
                    r = oh % 2
                    pa = PA[:, r * 512:(r + 1) * 512]
                    pb = PB[:, r * 512:(r + 1) * 512]
                    pak = [("PA", r)]
                    pbk = [("PB", r)]

                    def amm(E, ic=ic, oh=oh, pa=pa, ogi=ogi):
                        for ec in range(ND):
                            inst = E.matmul(
                                pa,
                                ogi[:, ec * CH + ic * 128:ec * CH + (ic + 1) * 128],
                                wpv[:, ec * D + oh * 512:ec * D + (oh + 1) * 512],
                                start=(ec == 0), stop=(ec == ND - 1))
                        return inst
                    if "amm" not in SKIP:
                        EM("PE", amm, reads=ogi_k + wp_keys, writes=pak)

                    def fmm(E, ic=ic, oh=oh, pb=pb, x8tv=x8tv):
                        for dc in range(ND):
                            inst = E.matmul(
                                pb,
                                x8tv[:, dc * CH + ic * 128:dc * CH + (ic + 1) * 128],
                                w28[:, dc * D + oh * 512:dc * D + (oh + 1) * 512],
                                start=(dc == 0), stop=(dc == ND - 1))
                        return inst
                    if "fmm" not in SKIP:
                        EM("PE", fmm, reads=x8tk + w28_k, writes=pbk)
                    if "evac" in SKIP:
                        continue
                    t1 = t1b[:, r * 512:(r + 1) * 512]
                    col = oh * 512
                    EM("ACT", lambda E, t1=t1, pb=pb: E.activation(
                        t1, pb, AF.Identity, scale=inv[:]),
                       reads=pbk + ["inv"], writes=[("t1b", r)])
                    EM("DVE", lambda E, t1=t1, pa=pa: E.tensor_add(t1, t1, pa),
                       reads=[("t1b", r)] + pak, writes=[("t1b", r)])
                    EM("DVE", lambda E, t1=t1, col=col, ov=ov, xrv=xrv:
                       E.tensor_add(ov[:, col:col + 512], t1,
                                    xrv[:, col:col + 512]),
                       reads=[("t1b", r)] + xres_k, writes=outb_k)
                if "outst" not in SKIP:
                    EM("SP", lambda E, c=c, ic=ic, ov=ov: E.dma_start(
                        out=out_d[c * CH + ic * 128:c * CH + (ic + 1) * 128, :],
                        in_=ov),
                       reads=outb_k, writes=[("out", c, ic)], dma=True)

        kb.enabled = True
        kb.barrier_wait("SP")
        kb.finalize(blk)
    return nc, kb.n_ops


def _host_inputs(x, Wq, Wk, Wv, Wg, Wp, W2, scale, scale1, n_tiles=8):
    n_tok = n_tiles * TT
    f32 = np.float32
    wq_t = np.ascontiguousarray(np.asarray(Wq, f32).T)
    wk_t = np.ascontiguousarray(np.asarray(Wk, f32).T)
    wv_t = np.ascontiguousarray(np.asarray(Wv, f32).T)
    wg_t = np.ascontiguousarray(np.asarray(Wg, f32).T)
    wp_t = np.ascontiguousarray(np.asarray(Wp, f32).T)
    w2 = np.ascontiguousarray(np.asarray(W2, f32))
    scv = np.full((128, 1), np.asarray(scale), f32)
    sc1v = np.full((128, 1), np.asarray(scale1), f32)
    jj = np.arange(256)
    m = (jj[:, None] <= jj[None, :]).astype(f32)
    masks = np.ascontiguousarray(
        np.concatenate([m[0:128, :], m[128:256, :]], axis=1))
    eye = np.eye(128, dtype=f32)
    x = np.asarray(x, f32)
    ins = []
    for c in range(8):
        b, h = c // 2, c % 2
        ins.append({
            "x": np.ascontiguousarray(x[b, h * n_tok:(h + 1) * n_tok, :]),
            "wq_t": wq_t, "wk_t": wk_t, "wv_t": wv_t, "wg_t": wg_t,
            "wp_t": wp_t, "w2": w2, "scv": scv, "sc1v": sc1v,
            "cgate": np.full((128, 1), float(h), f32),
            "masks": masks, "eye": eye,
        })
    return ins


def kernel(L0_attn_x, L0_attn_Wq, L0_attn_Wk, L0_attn_Wv, L0_attn_Wg,
           L0_attn_Wp, L0_ffn_input2, L0_ffn_scale, L0_ffn_scale1):
    ins = _host_inputs(L0_attn_x, L0_attn_Wq, L0_attn_Wk, L0_attn_Wv,
                       L0_attn_Wg, L0_attn_Wp, L0_ffn_input2,
                       L0_ffn_scale, L0_ffn_scale1)
    nc, _ = build_program()
    res = run_bass_kernel_spmd(nc, ins, core_ids=list(range(8)))
    out = np.empty((4, 8192, 2048), np.float32)
    for c in range(8):
        b, h = c // 2, c % 2
        out[b, h * 4096:(h + 1) * 4096, :] = res.results[c]["out"]
    return out
